# version 24 (final)
# baseline (speedup 1.0000x reference)
import numpy as np
import contextlib
import concourse.bass as bass
import concourse.mybir as mybir
from concourse.bass_utils import run_bass_kernel_spmd

F32 = mybir.dt.float32
BF16 = mybir.dt.bfloat16
AF = mybir.ActivationFunctionType
ALU = mybir.AluOpType
AX = mybir.AxisListType

S = 2048
D = 2048
NH = 16
NT = 16
DC = 16
NJ = 4
RK = 64
NE = 8
ER = 512
INVSQ = float(1.0 / np.sqrt(128.0))

ENGS = ("tensor", "vector", "scalar", "gpsimd", "sync")

DEBUG = 0


class Emit:

    def __init__(self, nc, sems):
        self.nc = nc
        self.q = {e: [] for e in ENGS}
        self.sems = sems
        self.cnt = {k: 0 for k in sems}
        self.step = {k: (16 if k.startswith("d") else 1) for k in sems}
        self.waited = {e: {} for e in ENGS}

    def op(self, eng, fn, inc=None, waits=()):
        waits = list(waits)
        if inc is not None and self.step.get(inc) == 16 and self.cnt[inc] > 0:
            waits.append((inc, self.cnt[inc]))
        ws = []
        for wt in waits:
            if wt is None:
                continue
            s, c = wt
            if c is None or c <= 0:
                continue
            if self.waited[eng].get(s, 0) >= c:
                continue
            self.waited[eng][s] = c
            ws.append((s, c))
        tgt = None
        if inc is not None:
            self.cnt[inc] += self.step[inc]
            tgt = self.cnt[inc]
        sems = self.sems
        step = self.step

        def closure(e, fn=fn, ws=tuple(ws), inc=inc):
            for s, c in ws:
                e.wait_ge(sems[s], c)
            r = fn(e)
            if inc is not None:
                r.then_inc(sems[inc], step[inc])

        self.q[eng].append(closure)
        return tgt

    def flush(self, block):
        for name in ENGS:
            if not self.q[name]:
                continue
            fns = self.q[name]
            self.q[name] = []

            def section(e, fns=fns):
                for f in fns:
                    f(e)

            getattr(block, name)(section)


def build_program(debug=0):
    nc = bass.Bass("TRN2", target_bir_lowering=False)

    x_d = nc.dram_tensor("x", [S, D], F32, kind="ExternalInput").ap()
    wq_d = nc.dram_tensor("wq", [NH, 128, D], F32, kind="ExternalInput").ap()
    wk_d = nc.dram_tensor("wk", [NH, 128, D], F32, kind="ExternalInput").ap()
    wv_d = nc.dram_tensor("wv", [NH, 128, D], F32, kind="ExternalInput").ap()
    wo_d = nc.dram_tensor("wo", [NH, D, 128], F32, kind="ExternalInput").ap()
    fa_d = nc.dram_tensor("fa", [ER, D], F32, kind="ExternalInput").ap()
    fb_d = nc.dram_tensor("fb", [NE, D, RK], F32, kind="ExternalInput").ap()
    wr_d = nc.dram_tensor("wr", [NE, D], F32, kind="ExternalInput").ap()
    br_dt = nc.dram_tensor("br", [1, NE], F32, kind="ExternalInput")
    br_d = br_dt.ap()
    out_d = nc.dram_tensor("out", [S, D], F32, kind="ExternalOutput").ap()
    if debug:
        dbg_xnT = nc.dram_tensor("dbg_xnT", [128, DC * S], BF16, kind="ExternalOutput").ap()
        dbg_zT = nc.dram_tensor("dbg_zT", [128, NH * S], BF16, kind="ExternalOutput").ap()
        dbg_exp = nc.dram_tensor("dbg_exp", [128, 4 * 512], BF16, kind="ExternalOutput").ap()
        dbg_rse = nc.dram_tensor("dbg_rse", [128, 512], F32, kind="ExternalOutput").ap()
        dbg_q = nc.dram_tensor("dbg_q", [128, 2 * 512], BF16, kind="ExternalOutput").ap()
        dbg_st = nc.dram_tensor("dbg_st", [128, 6 * NT], F32, kind="ExternalOutput").ap()
        dbg_x1 = nc.dram_tensor("dbg_x1", [128, D], F32, kind="ExternalOutput").ap()
        dbg_hm = nc.dram_tensor("dbg_hm", [128, ER + 3 * NE + 64], F32, kind="ExternalOutput").ap()
        dbg_xt = nc.dram_tensor("dbg_xt", [128, D], F32, kind="ExternalOutput").ap()
        dbg_ps1 = nc.dram_tensor("dbg_ps1", [128, 512], F32, kind="ExternalOutput").ap()
        dbg_sc = nc.dram_tensor("dbg_sc", [128, 512], F32, kind="ExternalOutput").ap()
        dbg_scr = es0 = None

    es = contextlib.ExitStack()
    esA = contextlib.ExitStack()
    esB = contextlib.ExitStack()
    esP1 = contextlib.ExitStack()
    with es:
        sems = {}
        for name in ("dS", "dG", "dO", "dA", "dV", "dB", "dC", "dF",
                     "T", "V", "Sc", "G"):
            sems[name] = es.enter_context(nc.semaphore(name))
        em = Emit(nc, sems)

        idb = es.enter_context(nc.sbuf_tensor("idb", [128, 128], BF16))
        onesq = es.enter_context(nc.sbuf_tensor("onesq", [128, 128], BF16))
        epsv = es.enter_context(nc.sbuf_tensor("epsv", [128, 1], F32))
        negC = es.enter_context(nc.sbuf_tensor("negC", [128, 1], F32))
        br_sb = es.enter_context(nc.sbuf_tensor("br_sb", [128, NE], F32))
        zT = es.enter_context(nc.sbuf_tensor("zT", [128, NH, S], BF16))
        st_sum = es.enter_context(nc.sbuf_tensor("st_sum", [128, NT], F32))
        st_sq = es.enter_context(nc.sbuf_tensor("st_sq", [128, NT], F32))
        st_mean = st_sum
        st_var = st_sq
        st_rstd = es.enter_context(nc.sbuf_tensor("st_rstd", [128, NT], F32))
        st_nm = es.enter_context(nc.sbuf_tensor("st_nm", [128, NT], F32))
        sq_box = [None]

        ps = [es.enter_context(nc.psum_tensor(f"ps{i}", [128, 512], F32)) for i in range(8)]
        bank_free = {i: None for i in range(8)}

        def bw(i):
            w = bank_free[i]
            return [w] if w is not None else []

        g1 = em.op("gpsimd", lambda g: g.memset(onesq[:], 1.0), inc="G")
        em.op("gpsimd", lambda g: g.memset(epsv[:], 1e-5))
        em.op("gpsimd", lambda g: g.memset(negC[:], -20.0))
        gid = em.op("gpsimd", lambda g: g.affine_select(
            idb[:], onesq[:], [[1, 128]], ALU.is_equal, 0.0,
            base=0, channel_multiplier=-1), inc="G", waits=[("G", g1)])
        brt = em.op("sync", lambda sy: sy.dma_start(
            out=br_sb[:], in_=bass.AP(br_dt, 0, [[0, 128], [1, NE]]),
            allow_slow_non_contiguous=True), inc="dS")

        def emit_ln(src_ap, t, out_ap, ready, extra_out_wait=None):
            c = 1.0 / float(D)
            v1 = em.op("vector", lambda v, t=t: v.reduce_sum(
                st_sum[:, t:t+1], src_ap, axis=AX.X), inc="V", waits=ready)
            stag = em.op("scalar", lambda s, t=t: s.activation(
                sq_box[0][:], src_ap, AF.Square, accum_out=st_sq[:, t:t+1]),
                inc="Sc", waits=ready)
            em.op("vector", lambda v, t=t: v.tensor_scalar(
                st_mean[:, t:t+1], st_sum[:, t:t+1], c, None, op0=ALU.mult),
                waits=[("Sc", stag), ("V", v1)])
            v2 = em.op("vector", lambda v, t=t: v.tensor_scalar(
                st_var[:, t:t+1], st_sq[:, t:t+1], c, None, op0=ALU.mult), inc="V")
            v3 = em.op("vector", lambda v, t=t: v.tensor_tensor(
                st_nm[:, t:t+1], st_mean[:, t:t+1], st_mean[:, t:t+1], op=ALU.mult),
                inc="V", waits=[("V", v2)])
            vtag = em.op("vector", lambda v, t=t: v.tensor_tensor(
                st_var[:, t:t+1], st_var[:, t:t+1], st_nm[:, t:t+1], op=ALU.subtract),
                inc="V", waits=[("V", v3)])
            sqt = em.op("scalar", lambda s, t=t: s.activation(
                st_rstd[:, t:t+1], st_var[:, t:t+1], AF.Sqrt, bias=epsv[:]),
                inc="Sc", waits=[("V", vtag), ("G", gid)])
            v4 = em.op("vector", lambda v, t=t: v.reciprocal(
                st_rstd[:, t:t+1], st_rstd[:, t:t+1]), inc="V", waits=[("Sc", sqt)])
            vtag2 = em.op("vector", lambda v, t=t: v.tensor_scalar(
                st_nm[:, t:t+1], st_mean[:, t:t+1], st_rstd[:, t:t+1], -1.0,
                op0=ALU.mult, op1=ALU.mult), inc="V", waits=[("V", v4)])
            wapply = [("V", vtag2)]
            if extra_out_wait is not None:
                wapply.append(extra_out_wait)
            atag = em.op("scalar", lambda s, t=t: s.activation(
                out_ap, src_ap, AF.Identity, bias=st_nm[:, t:t+1],
                scale=st_rstd[:, t:t+1]), inc="Sc", waits=wapply)
            return atag

        xnT = esA.enter_context(nc.sbuf_tensor("xnT", [128, DC, S], BF16))
        wstg = esA.enter_context(nc.sbuf_tensor("wstg", [128, 2, D], BF16))
        wqkvT = esA.enter_context(nc.sbuf_tensor("wqkvT", [128, 3, 2, DC, 128], BF16))
        xstage = esP1.enter_context(nc.sbuf_tensor("xstage", [128, 4, D], F32))
        xn_bf = esP1.enter_context(nc.sbuf_tensor("xn_bf", [128, 2, D], BF16))

        xslot_free = [None, None, None, None]
        xnslot_free = [None, None]
        NS = 4
        c = 1.0 / float(D)
        dS_tag = {}; sq_tag = {}; vchain_tag = {}; sqrt_tag = {}; nm2_tag = {}
        red_tag = {}
        apply_tag = {}; tt_tag = {}

        ROT = ("dA", "dB")

        def sA(i):
            sl = i % NS
            sem = ROT[i % 2]
            dS_tag[i] = (sem, em.op("scalar", lambda s, i=i, sl=sl: s.dma_start(
                out=xstage[:, sl, :], in_=x_d[i*128:(i+1)*128, :]),
                inc=sem, waits=[xslot_free[sl]]))

        def sB(i):
            sl = i % NS
            dA = dS_tag[i]
            red_tag[i] = em.op("vector", lambda v, i=i, sl=sl: v.reduce_sum(
                st_sum[:, i:i+1], xstage[:, sl, :], axis=AX.X),
                inc="V", waits=[dA])
            sq_tag[i] = em.op("scalar", lambda s, i=i, sl=sl: s.activation(
                xnT[:, :, i*128:(i+1)*128], xstage[:, sl, :], AF.Square,
                accum_out=st_sq[:, i:i+1]),
                inc="Sc", waits=[dA])

        def s23(i):
            m1 = em.op("vector", lambda v, i=i: v.tensor_scalar(
                st_mean[:, i:i+1], st_sum[:, i:i+1], c, None, op0=ALU.mult),
                inc="V", waits=[("Sc", sq_tag[i]), ("V", red_tag[i])])
            v2 = em.op("vector", lambda v, i=i: v.tensor_scalar(
                st_var[:, i:i+1], st_sq[:, i:i+1], c, None, op0=ALU.mult), inc="V")
            v3_ = em.op("vector", lambda v, i=i: v.tensor_tensor(
                st_nm[:, i:i+1], st_mean[:, i:i+1], st_mean[:, i:i+1], op=ALU.mult),
                inc="V", waits=[("V", m1)])
            vchain_tag[i] = em.op("vector", lambda v, i=i: v.tensor_tensor(
                st_var[:, i:i+1], st_var[:, i:i+1], st_nm[:, i:i+1], op=ALU.subtract),
                inc="V", waits=[("V", v3_)])
            sqrt_tag[i] = em.op("scalar", lambda s, i=i: s.activation(
                st_rstd[:, i:i+1], st_var[:, i:i+1], AF.Sqrt, bias=epsv[:]),
                inc="Sc", waits=[("V", vchain_tag[i]), ("G", gid)])

        def s45(i):
            v4_ = em.op("vector", lambda v, i=i: v.reciprocal(
                st_rstd[:, i:i+1], st_rstd[:, i:i+1]), inc="V",
                waits=[("Sc", sqrt_tag[i])])
            nm2_tag[i] = em.op("vector", lambda v, i=i: v.tensor_scalar(
                st_nm[:, i:i+1], st_mean[:, i:i+1], st_rstd[:, i:i+1], -1.0,
                op0=ALU.mult, op1=ALU.mult), inc="V", waits=[("V", v4_)])
            sl = i % NS
            nsl = i % 2
            wap = [("V", nm2_tag[i])]
            if xnslot_free[nsl]:
                wap.append(xnslot_free[nsl])
            apply_tag[i] = em.op("scalar", lambda s, i=i, sl=sl, nsl=nsl: s.activation(
                xn_bf[:, nsl, :], xstage[:, sl, :], AF.Identity,
                bias=st_nm[:, i:i+1], scale=st_rstd[:, i:i+1]),
                inc="Sc", waits=wap)
            xslot_free[sl] = ("Sc", apply_tag[i])

        def s6(i):
            nsl = i % 2
            tt = em.op("sync", lambda sy, i=i, nsl=nsl: sy.dma_start_transpose(
                xnT[:, :, i*128:(i+1)*128],
                xn_bf[:, nsl, :]),
                inc="dS", waits=[("Sc", apply_tag[i])])
            xnslot_free[nsl] = ("dS", tt)
            tt_tag[i] = tt

        for i in range(NT + 3):
            if i < NT:
                sA(i)
            if 0 <= i - 1 < NT:
                sB(i - 1)
            if 0 <= i - 2 < NT:
                s23(i - 2)
            if 0 <= i - 3 < NT:
                s45(i - 3)
                s6(i - 3)
        atag = apply_tag[NT - 1]
        xn_ready = ("dS", tt_tag[NT - 1])
        if debug:
            em.op("gpsimd", lambda g: g.dma_start(
                out=dbg_xnT, in_=xnT[:].rearrange("p c s -> p (c s)")),
                inc="dG", waits=[("V", em.cnt["V"])])

        if debug:
            em.op("gpsimd", lambda g: g.dma_start(out=dbg_st[:, 0:NT], in_=st_sum[:]),
                  inc="dG", waits=[("Sc", em.cnt["Sc"]), ("V", em.cnt["V"])])
            em.op("gpsimd", lambda g: g.dma_start(out=dbg_st[:, NT:2*NT], in_=st_sq[:]), inc="dG")
            em.op("gpsimd", lambda g: g.dma_start(out=dbg_st[:, 2*NT:3*NT], in_=st_mean[:]), inc="dG")
            em.op("gpsimd", lambda g: g.dma_start(out=dbg_st[:, 3*NT:4*NT], in_=st_var[:]), inc="dG")
            em.op("gpsimd", lambda g: g.dma_start(out=dbg_st[:, 4*NT:5*NT], in_=st_rstd[:]), inc="dG")
            em.op("gpsimd", lambda g: g.dma_start(out=dbg_st[:, 5*NT:6*NT], in_=st_nm[:]), inc="dG")
        p1_free = [("Sc", atag), xn_ready]
        block = es.enter_context(nc.Block())
        em.flush(block)
        esP1.close()

        stg_box = [None, None]
        qT = esB.enter_context(nc.sbuf_tensor("qT", [128, 2, S], BF16, side="right"))
        kT = esB.enter_context(nc.sbuf_tensor("kT", [128, 2, S], BF16, side="right"))
        vtm = esB.enter_context(nc.sbuf_tensor("vtm", [128, NT, 256], BF16, side="right"))
        expS = esB.enter_context(nc.sbuf_tensor("expS", [128, 3, 512], BF16, side="right"))
        expSm = esB.enter_context(nc.sbuf_tensor("expSm", [128, 2, 512], BF16, side="right"))
        rse = esB.enter_context(nc.sbuf_tensor("rse", [128, 512], F32, side="right"))
        stg2f = esB.enter_context(nc.sbuf_tensor("stg2f", [128, 3, 1024], F32, side="right"))
        stg2b = esB.enter_context(nc.sbuf_tensor("stg2b", [128, 2, 1024], BF16, side="right"))
        wd = {0: wq_d, 1: wk_d, 2: wv_d}
        wqkvT_free = [None, None, None]
        qkv_free = None
        last_xnT_use = None
        last_zT_write = None

        ds_base = em.cnt["dS"]
        def WXTAG(L):
            return ds_base + 16 * (L + 1)
        wload_tag = {}

        esA.close()
        esB.close()
        woT = es.enter_context(nc.sbuf_tensor("woT", [128, NH, D], BF16))
        fbB = es.enter_context(nc.sbuf_tensor("fbB", [128, 4, D], BF16))
        faT = es.enter_context(nc.sbuf_tensor("faT", [128, 4, DC, 128], BF16, side="right"))
        wrT = es.enter_context(nc.sbuf_tensor("wrT", [128, DC, NE], BF16, side="right"))
        xt = es.enter_context(nc.sbuf_tensor("xt", [128, 3, D], F32, side="right"))
        esS = contextlib.ExitStack()
        wr_stg = esS.enter_context(nc.sbuf_tensor("wr_stg", [8, 2048], BF16, side="right"))

        sc_box = [0]
        stream_k = [0]
        ld_tag = {}
        cast_tag = {}
        xbar_tag = {}

        def emit_stream_item():
            k = stream_k[0]
            stream_k[0] += 1
            sl = k % 3
            bsl = k % 2
            qs = "scalar" if k % 2 == 0 else "gpsimd"
            dsem = (("dA", "dB") if k % 2 == 0 else ("dV", "dC"))[(k // 2) % 2]
            xsem = "dS" if k % 2 == 0 else "dF"
            wld = [cast_tag[k - 3]] if k >= 3 else []
            if k < 32:
                q, h = k // 16, k % 16
                t = em.op(qs, lambda s, q=q, h=h, sl=sl: s.dma_start(
                    out=stg2f[:, sl, :].rearrange("p (c e) -> p c e", c=8),
                    in_=wo_d[h].rearrange("(c p) e -> p c e", p=128)[:, q*8:(q+1)*8, :]),
                    inc=dsem, waits=wld)
            elif k >= 40:
                er4, q = (k - 40) // 2, (k - 40) % 2
                t = em.op(qs, lambda s, er4=er4, q=q, sl=sl: s.dma_start(
                    out=stg2f[:, sl, :],
                    in_=fa_d[er4*128:(er4+1)*128, q*1024:(q+1)*1024]),
                    inc=dsem, waits=wld)
            else:
                kk = k - 32
                p_, q = kk // 2, kk % 2
                e0 = 2 * p_
                sv = stg2f[:, sl, :].rearrange("p (c a r) -> p c a r", c=8, a=2)
                em.op(qs, lambda s, sv=sv, e0=e0, q=q: s.dma_start(
                    out=sv[:, :, 0, :],
                    in_=fb_d[e0].rearrange("(c p) r -> p c r", p=128)[:, q*8:(q+1)*8, :]),
                    inc=dsem, waits=wld)
                t = em.op(qs, lambda s, sv=sv, e0=e0, q=q: s.dma_start(
                    out=sv[:, :, 1, :],
                    in_=fb_d[e0+1].rearrange("(c p) r -> p c r", p=128)[:, q*8:(q+1)*8, :]),
                    inc=dsem)
            ld_tag[k] = (dsem, t)
            wc = [ld_tag[k]]
            if k >= 2:
                wc.append(xbar_tag[k - 2])
            ct = em.op("vector", lambda v, sl=sl, bsl=bsl: v.tensor_copy(
                stg2b[:, bsl, :], stg2f[:, sl, :]), inc="V", waits=wc)
            cast_tag[k] = ("V", ct)
            wx = [("V", ct)]
            if k < 32:
                q, h = k // 16, k % 16
                wx.append(("T", last_xnT_use))
                xb = em.op("sync", lambda sy, q=q, h=h, bsl=bsl: sy.dma_start_transpose(
                    woT[:, h, q*1024:(q+1)*1024].rearrange("p (c e) -> p c e", c=8),
                    stg2b[:, bsl, :]), inc=xsem, waits=wx)
            elif k >= 40:
                er4, q = (k - 40) // 2, (k - 40) % 2
                wx.append(("T", sc_box[0]))
                xb = em.op("sync", lambda sy, er4=er4, q=q, bsl=bsl: sy.dma_start_transpose(
                    faT[:, er4, q*8:(q+1)*8, :], stg2b[:, bsl, :]),
                    inc=xsem, waits=wx)
            else:
                kk = k - 32
                p_, q = kk // 2, kk % 2
                wx += [("T", last_xnT_use), ("dS", WXTAG(47))]
                xb = em.op("sync", lambda sy, p_=p_, q=q, bsl=bsl: sy.dma_start_transpose(
                    fbB[:, p_, q*1024:(q+1)*1024].rearrange("p (c e) -> p c e", c=8),
                    stg2b[:, bsl, :]), inc=xsem, waits=wx)
            xbar_tag[k] = (xsem, xb)

        def emit_wloads(g_hg):
            for k in range(6):
                mi, h2 = k // 2, k % 2
                L = 6 * g_hg + k
                sl = L % 2
                w = [("dS", WXTAG(L - 2))] if L >= 2 else []
                wload_tag[(g_hg, mi, h2)] = em.op(
                    "gpsimd", lambda g, mi=mi, h2=h2, sl=sl, g_hg=g_hg:
                    g.dma_start(out=wstg[:, sl, :], in_=wd[mi][2*g_hg+h2]),
                    inc="dG", waits=w)

        emit_wloads(0)
        for hg in range(8):
            h0 = 2 * hg
            tp_tags = {}
            for k in range(6):
                mi, h2 = k // 2, k % 2
                L = 6 * hg + k
                sl = L % 2
                wv_ = [("dG", wload_tag[(hg, mi, h2)])]
                if wqkvT_free[mi] is not None and h2 == 0:
                    wv_.append(wqkvT_free[mi])
                ttag = em.op("sync", lambda sy, mi=mi, h2=h2, sl=sl:
                             sy.dma_start_transpose(
                                 wqkvT[:, mi, h2, :, :],
                                 wstg[:, sl, :]),
                             inc="dS", waits=wv_)
                assert ttag == WXTAG(L), (ttag, WXTAG(L))
                tp_tags[mi] = ("dS", ttag)
            if hg < 7:
                emit_wloads(hg + 1)
            else:
                pass

            qk_evac = {}
            for mi in range(2):
                dst = qT if mi == 0 else kT
                for h2 in range(2):
                    for j in range(NJ):
                        b = j % 2
                        mm = None
                        for dc in range(DC):
                            w = (bw(b) + [tp_tags[mi], qkv_free, xn_ready]) if dc == 0 else []
                            mm = em.op("tensor", lambda tn, mi=mi, h2=h2, j=j, dc=dc, b=b:
                                       tn.matmul(ps[b][:, 0:512],
                                                 wqkvT[:, mi, h2, dc, :],
                                                 xnT[:, dc, j*512:(j+1)*512],
                                                 start=(dc == 0), stop=(dc == DC-1)),
                                       inc="T" if dc == DC-1 else None, waits=w)
                        last_xnT_use = mm
                        wqkvT_free[mi] = ("T", mm)
                        vt = em.op("vector", lambda v, dst=dst, h2=h2, j=j, b=b:
                                   v.tensor_copy(dst[:, h2, j*512:(j+1)*512],
                                                 ps[b][:, 0:512]),
                                   inc="V", waits=[("T", mm)])
                        bank_free[b] = ("V", vt)
                        qk_evac[(mi, h2, j)] = vt
            v_evac = {}
            for mt in range(NT):
                b = mt % 2
                mm = None
                for dc in range(DC):
                    w = (bw(b) + [tp_tags[2], qkv_free, xn_ready]) if dc == 0 else []
                    mm = em.op("tensor", lambda tn, mt=mt, dc=dc, b=b:
                               tn.matmul(ps[b][:, 0:256],
                                         xnT[:, dc, mt*128:(mt+1)*128],
                                         wqkvT[:, 2, :, dc, :],
                                         start=(dc == 0), stop=(dc == DC-1)),
                               inc="T" if dc == DC-1 else None, waits=w)
                last_xnT_use = mm
                wqkvT_free[2] = ("T", mm)
                vt = em.op("vector", lambda v, mt=mt, b=b:
                           v.tensor_copy(vtm[:, mt, :], ps[b][:, 0:256]),
                           inc="V", waits=[("T", mm)])
                bank_free[b] = ("V", vt)
                v_evac[mt] = vt

            for h2 in range(2):
                h = h0 + h2
                for j in range(NJ):
                    if hg == 7:
                        for _ in range(5):
                            emit_stream_item()
                    nmc = 4 * (j + 1)
                    bZ = 4 + (j % 2)
                    bSE = 6 + (j % 2)
                    slot_rel = {}
                    se_tags = {}
                    pv_last = None
                    sbanks = [2, 3, 0, 1]
                    stags = {}

                    def emit_score(mc, h2=h2, j=j):
                        bS = sbanks[mc % 4]
                        w = bw(bS)
                        if mc == 0:
                            w += [("V", qk_evac[(0, h2, j)]), ("V", qk_evac[(1, h2, j)])]
                        stags[mc] = (bS, em.op(
                            "tensor", lambda tn, h2=h2, j=j, mc=mc, bS=bS:
                            tn.matmul(ps[bS][:, 0:512],
                                      kT[:, h2, mc*128:(mc+1)*128],
                                      qT[:, h2, j*512:(j+1)*512],
                                      start=True, stop=True),
                            inc="T", waits=w))
                        if hg == 7:
                            sc_box[0] = stags[mc][1]

                    for pre in range(min(3, nmc)):
                        emit_score(pre)
                    for mc in range(nmc):
                        bS, stag = stags[mc]
                        esl = mc % 3
                        we = [("T", stag)]
                        if mc >= 3:
                            we.append(slot_rel[mc - 3])
                        et = em.op("scalar", lambda s, bS=bS, esl=esl:
                                   s.activation(expS[:, esl, :], ps[bS][:, 0:512],
                                                AF.Exp, scale=INVSQ, bias=negC[:]),
                                   inc="Sc", waits=we + [("G", gid)])
                        bank_free[bS] = ("Sc", et)
                        diag = (mc >= 4 * j)
                        if diag:
                            msl = mc % 2
                            wg = [("Sc", et)]
                            if mc - 2 >= 4 * j:
                                wg.append(("T", se_tags[mc - 2]))
                            gt = em.op("gpsimd", lambda g, esl=esl, msl=msl, j=j, mc=mc:
                                       g.affine_select(expSm[:, msl, :], expS[:, esl, :],
                                                       [[1, 512]], ALU.is_ge, 0.0,
                                                       base=512*j - 128*mc,
                                                       channel_multiplier=-1),
                                       inc="G", waits=wg)
                            ready = ("G", gt)
                            srcap = expSm[:, msl, :]
                            slot_rel[mc] = ("G", gt)
                        else:
                            ready = ("Sc", et)
                            srcap = expS[:, esl, :]
                            slot_rel[mc] = None
                        if debug and hg == 0 and h2 == 0 and j == 0:
                            dsem = em.op("gpsimd", lambda g, srcap=srcap, mc=mc:
                                         g.dma_start(out=dbg_exp[:, mc*512:(mc+1)*512],
                                                     in_=srcap), inc="dG", waits=[ready])
                        if mc + 3 < nmc:
                            emit_score(mc + 3)
                        wpv = [ready, ("V", v_evac[mc])]
                        if mc == 0:
                            wpv += bw(bZ)
                        pv_last = em.op("tensor",
                                        lambda tn, h2=h2, mc=mc, bZ=bZ, srcap=srcap, nmc=nmc:
                                        tn.matmul(ps[bZ][:, 0:512],
                                                  vtm[:, mc, h2*128:(h2+1)*128],
                                                  srcap,
                                                  start=(mc == 0), stop=(mc == nmc-1)),
                                        inc="T", waits=wpv)
                        wse = bw(bSE) if mc == 0 else []
                        se = em.op("tensor", lambda tn, mc=mc, bSE=bSE, srcap=srcap, nmc=nmc:
                                   tn.matmul(ps[bSE][:, 0:512],
                                             onesq[:],
                                             srcap,
                                             start=(mc == 0), stop=(mc == nmc-1)),
                                   inc="T", waits=wse)
                        se_tags[mc] = se
                        if not diag:
                            slot_rel[mc] = ("T", se)
                    wrt = [("T", se_tags[nmc-1])]
                    if last_zT_write is not None:
                        wrt.append(("V", last_zT_write))
                    rt = em.op("vector", lambda v, bSE=bSE: v.reciprocal(
                        rse[:], ps[bSE][:, 0:512]),
                        inc="V", waits=wrt)
                    if debug and hg == 0 and h2 == 0 and j == 0:
                        em.op("gpsimd", lambda g: g.dma_start(
                            out=dbg_rse, in_=rse[:]), inc="dG",
                            waits=[("V", rt)])
                    bank_free[bSE] = ("V", rt)
                    em.op("vector", lambda v: v.engine_nop())
                    zt = em.op("vector", lambda v, h=h, j=j, bZ=bZ: v.tensor_tensor(
                        zT[:, h, j*512:(j+1)*512], ps[bZ][:, 0:512],
                        rse[:], op=ALU.mult),
                        inc="V", waits=[("T", pv_last), ("V", rt)])
                    bank_free[bZ] = ("V", zt)
                    last_zT_write = zt
            qkv_free = None

        tag_A_free = ("T", last_xnT_use)
        p2_T_final = em.cnt["T"]
        p2_Sc_final = em.cnt["Sc"]
        tag_B_free = ("V", em.cnt["V"])
        if debug:
            em.op("gpsimd", lambda g: g.dma_start(
                out=dbg_zT, in_=zT[:].rearrange("p c s -> p (c s)")),
                inc="dG", waits=[("V", em.cnt["V"])])

        em.flush(block)

        while stream_k[0] < 48:
            emit_stream_item()
        wo_ready_q = [xbar_tag[15], xbar_tag[31]]
        fb_ready = xbar_tag[39]
        fa_ready = xbar_tag[47]

        early_x = {}
        for stq in range(2):
            early_x[stq] = em.op("gpsimd", lambda g, stq=stq: g.dma_start(
                out=xt[:, stq, :], in_=x_d[stq*128:(stq+1)*128, :]),
                inc="dG", waits=[tag_B_free, xbar_tag[46], xbar_tag[47]])

        wr_dt = em.op("gpsimd", lambda g: g.dma_start(
            out=wr_stg[:], in_=wr_d), inc="dG",
            waits=[xbar_tag[46], xbar_tag[47]])
        pvw = ps[2]
        ttw = None
        for dc in range(DC):
            w2 = (bw(2) + [("dG", wr_dt), ("G", gid)]) if dc == 0 else []
            ttw = em.op("tensor", lambda tn, dc=dc: tn.transpose(
                pvw[:, dc*4:dc*4+4].bitcast(BF16),
                wr_stg[:, dc*128:(dc+1)*128], idb[0:8, 0:8]),
                inc="T" if dc == DC-1 else None, waits=w2)
        vtw = em.op("vector", lambda v: v.tensor_copy(
            wrT[:], pvw[:, 0:64].bitcast(BF16).rearrange("p (c e) -> p c e", c=DC)),
            inc="V", waits=[("T", ttw)])
        bank_free[2] = ("V", vtw)
        wr_ready = ("V", vtw)

        em.flush(block)
        esS.close()
        p3_region = [("T", ttw)]

        g_bf = es.enter_context(nc.sbuf_tensor("g_bf", [128, D], BF16, side="right"))
        gT = es.enter_context(nc.sbuf_tensor("gT", [128, 2, DC, 128], BF16, side="right"))
        lg = es.enter_context(nc.sbuf_tensor("lg", [128, 2, NE], F32, side="right"))
        expl = es.enter_context(nc.sbuf_tensor("expl", [128, NE], F32, side="right"))
        lmax = es.enter_context(nc.sbuf_tensor("lmax", [128, 1], F32, side="right"))
        emax = es.enter_context(nc.sbuf_tensor("emax", [128, 1], F32, side="right"))
        sume = es.enter_context(nc.sbuf_tensor("sume", [128, 1], F32, side="right"))
        wsc = es.enter_context(nc.sbuf_tensor("wsc", [128, 1], F32, side="right"))
        maskw = es.enter_context(nc.sbuf_tensor("maskw", [128, 2, NE], F32, side="right"))
        hm_bf = es.enter_context(nc.sbuf_tensor("hm_bf", [128, ER], BF16, side="right"))
        hmT = es.enter_context(nc.sbuf_tensor("hmT", [128, 4, 128], BF16, side="right"))

        xt_free = [None, None, None]
        g_bf_free = None
        gT_free = [None, None]
        hm_free = None
        hmT_free = None

        zt_ready = ("V", last_zT_write)
        attn_mm = {}
        attn_dma = {}
        attn_evac_last = {}
        out_last = {}

        def emit_attn_T(st, dmcs=(0, 1, 2, 3)):
            sl = st % 3
            if dmcs[0] == 0:
                if st < 2:
                    attn_dma[st] = early_x[st]
                else:
                    wx = [xt_free[sl]]
                    attn_dma[st] = em.op("gpsimd", lambda g, st=st, sl=sl: g.dma_start(
                        out=xt[:, sl, :], in_=x_d[st*128:(st+1)*128, :]),
                        inc="dG", waits=wx)
            for dmc in dmcs:
                b = dmc
                mm = None
                for h in range(NH):
                    w = (bw(b) + [zt_ready, wo_ready_q[dmc // 2]]) if h == 0 else []
                    mm = em.op("tensor", lambda tn, st=st, dmc=dmc, h=h, b=b:
                               tn.matmul(ps[b][:, 0:512],
                                         zT[:, h, st*128:(st+1)*128],
                                         woT[:, h, dmc*512:(dmc+1)*512],
                                         start=(h == 0), stop=(h == NH-1)),
                               inc="T" if h == NH-1 else None, waits=w)
                attn_mm[(st, dmc)] = mm

        def emit_attn_evacs(st):
            sl = st % 3
            ev = None
            for dmc in range(4):
                b = dmc
                ev = em.op("vector", lambda v, sl=sl, dmc=dmc, b=b:
                           v.tensor_tensor(xt[:, sl, dmc*512:(dmc+1)*512],
                                           ps[b][:, 0:512],
                                           xt[:, sl, dmc*512:(dmc+1)*512],
                                           op=ALU.add),
                           inc="V", waits=[("T", attn_mm[(st, dmc)]),
                                           ("dG", attn_dma[st])])
                bank_free[b] = ("V", ev)
            attn_evac_last[st] = ("V", ev)

        gT_ready = {}
        mw_tag = {}

        def emit_front(st):
            nonlocal g_bf_free
            sl = st % 3
            gsl = st % 2
            sq_box[0] = g_bf
            x1_ready = [attn_evac_last[st]]
            if st == 0:
                x1_ready += p3_region
            if g_bf_free:
                x1_ready = x1_ready + [g_bf_free]
            atag = emit_ln(xt[:, sl, :], st, g_bf[:], x1_ready,
                           extra_out_wait=g_bf_free)
            wg_ = [("Sc", atag)]
            if gT_free[gsl]:
                wg_.append(gT_free[gsl])
            elif st == 0:
                wg_ += p3_region
            gvt = em.op("sync", lambda sy, gsl=gsl: sy.dma_start_transpose(
                gT[:, gsl, :, :], g_bf[:]), inc="dS", waits=wg_)
            g_bf_free = ("dS", gvt)
            gT_ready[st] = ("dS", gvt)
            mm = None
            for dc in range(DC):
                w = (bw(4) + [gT_ready[st], wr_ready]) if dc == 0 else []
                mm = em.op("tensor", lambda tn, dc=dc, gsl=gsl:
                           tn.matmul(ps[4][:, 0:NE], gT[:, gsl, dc, :], wrT[:, dc, :],
                                     start=(dc == 0), stop=(dc == DC-1)),
                           inc="T" if dc == DC-1 else None, waits=w)
            msl = st % 2
            vtag = em.op("vector", lambda v, msl=msl: v.tensor_tensor(
                lg[:, msl, :], ps[4][:, 0:NE],
                br_sb[:], op=ALU.add),
                inc="V", waits=[("T", mm), ("dS", brt)])
            bank_free[4] = ("V", vtag)
            vl = em.op("vector", lambda v, msl=msl: v.tensor_reduce(
                lmax[:], lg[:, msl, :], axis=AX.X, op=ALU.max), inc="V",
                waits=[("V", vtag)])
            etag = em.op("scalar", lambda s, msl=msl: s.activation(
                expl[:], lg[:, msl, :], AF.Exp, bias=negC[:]), inc="Sc",
                waits=[("V", vl), ("G", gid)])
            v5 = em.op("vector", lambda v: v.reduce_sum(
                sume[:], expl[:], axis=AX.X), inc="V", waits=[("Sc", etag)])
            v6 = em.op("vector", lambda v: v.tensor_reduce(
                emax[:], expl[:], axis=AX.X, op=ALU.max), inc="V", waits=[("V", v5)])
            v7 = em.op("vector", lambda v: v.reciprocal(sume[:], sume[:]),
                       inc="V", waits=[("V", v6)])
            v8 = em.op("vector", lambda v: v.tensor_scalar(
                wsc[:], emax[:], sume[:], 1.0/64.0, op0=ALU.mult, op1=ALU.mult),
                inc="V", waits=[("V", v7)])
            mw = em.op("vector", lambda v, msl=msl: v.tensor_scalar(
                maskw[:, msl, :], lg[:, msl, :], lmax[:], wsc[:],
                op0=ALU.is_ge, op1=ALU.mult),
                inc="V", waits=[("V", v8)])
            mw_tag[st] = ("V", mw)

        def emit_back(st):
            nonlocal hm_free, hmT_free
            sl = st % 3
            gsl = st % 2
            msl = st % 2
            mmh = None
            for dc in range(DC):
                w = (bw(5) + [gT_ready[st], fa_ready]) if dc == 0 else []
                mmh = em.op("tensor", lambda tn, dc=dc, gsl=gsl:
                            tn.matmul(ps[5][:, 0:ER], gT[:, gsl, dc, :],
                                      faT[:, :, dc, :],
                                      start=(dc == 0), stop=(dc == DC-1)),
                            inc="T" if dc == DC-1 else None, waits=w)
            gT_free[gsl] = ("T", mmh)
            hev = None
            for e in range(NE):
                w = [("T", mmh), mw_tag[st]]
                if e == 0 and hm_free is not None:
                    w.append(hm_free)
                hev = em.op("vector", lambda v, e=e, msl=msl: v.tensor_scalar(
                    hm_bf[:, e*64:(e+1)*64], ps[5][:, e*64:(e+1)*64],
                    maskw[:, msl, e:e+1], None, op0=ALU.mult),
                    inc="V" if e == NE-1 else None, waits=w)
            bank_free[5] = ("V", hev)
            tt = None
            for k in range(4):
                w2 = (bw(4) + [("V", hev)] + ([hmT_free] if hmT_free else [])) if k == 0 else []
                tt = em.op("tensor", lambda tn, k=k:
                           tn.transpose(ps[4][:, k*64:k*64+64].bitcast(BF16),
                                        hm_bf[:, k*128:(k+1)*128], idb[:]),
                           inc="T" if k == 3 else None, waits=w2)
            hm_free = ("T", tt)
            hvt = em.op("vector", lambda v: v.tensor_copy(
                hmT[:], ps[4][:, 0:256].bitcast(BF16).rearrange("p (c e) -> p c e", c=4)),
                inc="V", waits=[("T", tt)])
            bank_free[4] = ("V", hvt)
            hmT_ready = ("V", hvt)
            out_tags = []
            mm = None
            for dmc in range(4):
                b = 6 + dmc % 2
                for erc in range(4):
                    w = (bw(b) + [hmT_ready, fb_ready]) if erc == 0 else []
                    mm = em.op("tensor", lambda tn, dmc=dmc, erc=erc, b=b:
                               tn.matmul(ps[b][:, 0:512],
                                         hmT[:, erc, :],
                                         fbB[:, erc, dmc*512:(dmc+1)*512],
                                         start=(erc == 0), stop=(erc == 3)),
                               inc="T" if erc == 3 else None, waits=w)
                yev = em.op("vector", lambda v, sl=sl, dmc=dmc, b=b:
                            v.tensor_tensor(xt[:, sl, dmc*512:(dmc+1)*512],
                                            ps[b][:, 0:512],
                                            xt[:, sl, dmc*512:(dmc+1)*512],
                                            op=ALU.add),
                            inc="V", waits=[("T", mm)])
                bank_free[b] = ("V", yev)
                ot = em.op("gpsimd", lambda g, st=st, sl=sl, dmc=dmc: g.dma_start(
                    out=out_d[st*128:(st+1)*128, dmc*512:(dmc+1)*512],
                    in_=xt[:, sl, dmc*512:(dmc+1)*512]),
                    inc="dO", waits=[("V", yev)])
                out_tags.append(ot)
            hmT_free = ("T", mm)
            xt_free[sl] = ("dO", out_tags[-1])

        for k in range(NT + 1):
            if k < NT:
                emit_attn_T(k)
            if 1 <= k <= NT:
                emit_front(k - 1)
                emit_back(k - 1)
            if k < NT:
                emit_attn_evacs(k)

        final_dO = em.cnt["dO"]
        em.op("scalar", lambda s: s.nop(), waits=[("dO", final_dO)])
        em.op("sync", lambda sy: sy.nop(), waits=[("dO", final_dO)])
        em.flush(block)

    return nc


def kernel(L0_attn_x, L0_attn_W_Q, L0_attn_W_K, L0_attn_W_V, L0_attn_W_O,
           L0_ffn_A, L0_ffn_B, L0_ffn_Wr, L0_ffn_br):
    nc = build_program(debug=DEBUG)
    x = np.ascontiguousarray(L0_attn_x, dtype=np.float32)
    fa = np.ascontiguousarray(np.asarray(L0_ffn_A, dtype=np.float32).reshape(ER, D))
    shared = {
        "wq": np.ascontiguousarray(L0_attn_W_Q, dtype=np.float32),
        "wk": np.ascontiguousarray(L0_attn_W_K, dtype=np.float32),
        "wv": np.ascontiguousarray(L0_attn_W_V, dtype=np.float32),
        "wo": np.ascontiguousarray(L0_attn_W_O, dtype=np.float32),
        "fa": fa,
        "fb": np.ascontiguousarray(L0_ffn_B, dtype=np.float32),
        "wr": np.ascontiguousarray(L0_ffn_Wr, dtype=np.float32),
        "br": np.ascontiguousarray(np.asarray(L0_ffn_br, dtype=np.float32).reshape(1, NE)),
    }
    in_maps = [dict(shared, x=np.ascontiguousarray(x[c])) for c in range(8)]
    res = run_bass_kernel_spmd(nc, in_maps, core_ids=list(range(8)))
    return np.stack([r["out"] for r in res.results], axis=0)


# version 25
# speedup vs baseline: 1.0756x; 1.0756x over previous
import numpy as np
import contextlib
import concourse.bass as bass
import concourse.mybir as mybir
from concourse.bass_utils import run_bass_kernel_spmd

F32 = mybir.dt.float32
BF16 = mybir.dt.bfloat16
AF = mybir.ActivationFunctionType
ALU = mybir.AluOpType
AX = mybir.AxisListType

S = 2048
D = 2048
NH = 16
NT = 16
DC = 16
NJ = 4
RK = 64
NE = 8
ER = 512
INVSQ = float(1.0 / np.sqrt(128.0))

ENGS = ("tensor", "vector", "scalar", "gpsimd", "sync")

DEBUG = 0


class Emit:

    def __init__(self, nc, sems):
        self.nc = nc
        self.q = {e: [] for e in ENGS}
        self.sems = sems
        self.cnt = {k: 0 for k in sems}
        self.step = {k: (16 if k.startswith("d") else 1) for k in sems}
        self.waited = {e: {} for e in ENGS}

    def op(self, eng, fn, inc=None, waits=()):
        waits = list(waits)
        if inc is not None and self.step.get(inc) == 16 and self.cnt[inc] > 0:
            waits.append((inc, self.cnt[inc]))
        ws = []
        for wt in waits:
            if wt is None:
                continue
            s, c = wt
            if c is None or c <= 0:
                continue
            if self.waited[eng].get(s, 0) >= c:
                continue
            self.waited[eng][s] = c
            ws.append((s, c))
        tgt = None
        if inc is not None:
            self.cnt[inc] += self.step[inc]
            tgt = self.cnt[inc]
        sems = self.sems
        step = self.step

        def closure(e, fn=fn, ws=tuple(ws), inc=inc):
            for s, c in ws:
                e.wait_ge(sems[s], c)
            r = fn(e)
            if inc is not None:
                r.then_inc(sems[inc], step[inc])

        self.q[eng].append(closure)
        return tgt

    def flush(self, block):
        for name in ENGS:
            if not self.q[name]:
                continue
            fns = self.q[name]
            self.q[name] = []

            def section(e, fns=fns):
                for f in fns:
                    f(e)

            getattr(block, name)(section)


def build_program(debug=0):
    nc = bass.Bass("TRN2", target_bir_lowering=False)

    x_d = nc.dram_tensor("x", [S, D], F32, kind="ExternalInput").ap()
    wq_d = nc.dram_tensor("wq", [NH, 128, D], F32, kind="ExternalInput").ap()
    wk_d = nc.dram_tensor("wk", [NH, 128, D], F32, kind="ExternalInput").ap()
    wv_d = nc.dram_tensor("wv", [NH, 128, D], F32, kind="ExternalInput").ap()
    wo_d = nc.dram_tensor("wo", [NH, D, 128], F32, kind="ExternalInput").ap()
    fa_d = nc.dram_tensor("fa", [ER, D], F32, kind="ExternalInput").ap()
    fb_d = nc.dram_tensor("fb", [NE, D, RK], F32, kind="ExternalInput").ap()
    wr_d = nc.dram_tensor("wr", [NE, D], F32, kind="ExternalInput").ap()
    br_dt = nc.dram_tensor("br", [1, NE], F32, kind="ExternalInput")
    br_d = br_dt.ap()
    out_d = nc.dram_tensor("out", [S, D], F32, kind="ExternalOutput").ap()
    if debug:
        dbg_xnT = nc.dram_tensor("dbg_xnT", [128, DC * S], BF16, kind="ExternalOutput").ap()
        dbg_zT = nc.dram_tensor("dbg_zT", [128, NH * S], BF16, kind="ExternalOutput").ap()
        dbg_exp = nc.dram_tensor("dbg_exp", [128, 4 * 512], BF16, kind="ExternalOutput").ap()
        dbg_rse = nc.dram_tensor("dbg_rse", [128, 512], F32, kind="ExternalOutput").ap()
        dbg_q = nc.dram_tensor("dbg_q", [128, 2 * 512], BF16, kind="ExternalOutput").ap()
        dbg_st = nc.dram_tensor("dbg_st", [128, 6 * NT], F32, kind="ExternalOutput").ap()
        dbg_x1 = nc.dram_tensor("dbg_x1", [128, D], F32, kind="ExternalOutput").ap()
        dbg_hm = nc.dram_tensor("dbg_hm", [128, ER + 3 * NE + 64], F32, kind="ExternalOutput").ap()
        dbg_xt = nc.dram_tensor("dbg_xt", [128, D], F32, kind="ExternalOutput").ap()
        dbg_ps1 = nc.dram_tensor("dbg_ps1", [128, 512], F32, kind="ExternalOutput").ap()
        dbg_sc = nc.dram_tensor("dbg_sc", [128, 512], F32, kind="ExternalOutput").ap()
        dbg_scr = es0 = None

    es = contextlib.ExitStack()
    esA = contextlib.ExitStack()
    esB = contextlib.ExitStack()
    esP1 = contextlib.ExitStack()
    with es:
        sems = {}
        for name in ("dS", "dG", "dO", "dA", "dV", "dB", "dC", "dF",
                     "T", "V", "Sc", "G"):
            sems[name] = es.enter_context(nc.semaphore(name))
        em = Emit(nc, sems)

        idb = es.enter_context(nc.sbuf_tensor("idb", [128, 128], BF16))
        onesq = es.enter_context(nc.sbuf_tensor("onesq", [128, 128], BF16))
        epsv = es.enter_context(nc.sbuf_tensor("epsv", [128, 1], F32))
        negC = es.enter_context(nc.sbuf_tensor("negC", [128, 1], F32))
        br_sb = es.enter_context(nc.sbuf_tensor("br_sb", [128, NE], F32))
        zT = es.enter_context(nc.sbuf_tensor("zT", [128, NH, S], BF16))
        st_sum = es.enter_context(nc.sbuf_tensor("st_sum", [128, NT], F32))
        st_sq = es.enter_context(nc.sbuf_tensor("st_sq", [128, NT], F32))
        st_mean = st_sum
        st_var = st_sq
        st_rstd = es.enter_context(nc.sbuf_tensor("st_rstd", [128, NT], F32))
        st_nm = es.enter_context(nc.sbuf_tensor("st_nm", [128, NT], F32))
        sq_box = [None]

        ps = [es.enter_context(nc.psum_tensor(f"ps{i}", [128, 512], F32)) for i in range(8)]
        bank_free = {i: None for i in range(8)}

        def bw(i):
            w = bank_free[i]
            return [w] if w is not None else []

        g1 = em.op("gpsimd", lambda g: g.memset(onesq[:], 1.0), inc="G")
        em.op("gpsimd", lambda g: g.memset(epsv[:], 1e-5))
        em.op("gpsimd", lambda g: g.memset(negC[:], -20.0))
        gid = em.op("gpsimd", lambda g: g.affine_select(
            idb[:], onesq[:], [[1, 128]], ALU.is_equal, 0.0,
            base=0, channel_multiplier=-1), inc="G", waits=[("G", g1)])
        brt = em.op("sync", lambda sy: sy.dma_start(
            out=br_sb[:], in_=bass.AP(br_dt, 0, [[0, 128], [1, NE]]),
            allow_slow_non_contiguous=True), inc="dS")

        def emit_ln(src_ap, t, out_ap, ready, extra_out_wait=None):
            c = 1.0 / float(D)
            v1 = em.op("vector", lambda v, t=t: v.reduce_sum(
                st_sum[:, t:t+1], src_ap, axis=AX.X), inc="V", waits=ready)
            stag = em.op("scalar", lambda s, t=t: s.activation(
                sq_box[0][:], src_ap, AF.Square, accum_out=st_sq[:, t:t+1]),
                inc="Sc", waits=ready)
            em.op("vector", lambda v, t=t: v.tensor_scalar(
                st_mean[:, t:t+1], st_sum[:, t:t+1], c, None, op0=ALU.mult),
                waits=[("Sc", stag), ("V", v1)])
            v2 = em.op("vector", lambda v, t=t: v.tensor_scalar(
                st_var[:, t:t+1], st_sq[:, t:t+1], c, None, op0=ALU.mult), inc="V")
            v3 = em.op("vector", lambda v, t=t: v.tensor_tensor(
                st_nm[:, t:t+1], st_mean[:, t:t+1], st_mean[:, t:t+1], op=ALU.mult),
                inc="V", waits=[("V", v2)])
            vtag = em.op("vector", lambda v, t=t: v.tensor_tensor(
                st_var[:, t:t+1], st_var[:, t:t+1], st_nm[:, t:t+1], op=ALU.subtract),
                inc="V", waits=[("V", v3)])
            sqt = em.op("scalar", lambda s, t=t: s.activation(
                st_rstd[:, t:t+1], st_var[:, t:t+1], AF.Sqrt, bias=epsv[:]),
                inc="Sc", waits=[("V", vtag), ("G", gid)])
            v4 = em.op("vector", lambda v, t=t: v.reciprocal(
                st_rstd[:, t:t+1], st_rstd[:, t:t+1]), inc="V", waits=[("Sc", sqt)])
            vtag2 = em.op("vector", lambda v, t=t: v.tensor_scalar(
                st_nm[:, t:t+1], st_mean[:, t:t+1], st_rstd[:, t:t+1], -1.0,
                op0=ALU.mult, op1=ALU.mult), inc="V", waits=[("V", v4)])
            wapply = [("V", vtag2)]
            if extra_out_wait is not None:
                wapply.append(extra_out_wait)
            atag = em.op("scalar", lambda s, t=t: s.activation(
                out_ap, src_ap, AF.Identity, bias=st_nm[:, t:t+1],
                scale=st_rstd[:, t:t+1]), inc="Sc", waits=wapply)
            return atag

        xnT = esA.enter_context(nc.sbuf_tensor("xnT", [128, DC, S], BF16))
        wstg = esA.enter_context(nc.sbuf_tensor("wstg", [128, 2, D], BF16))
        wqkvT = esA.enter_context(nc.sbuf_tensor("wqkvT", [128, 3, 2, DC, 128], BF16))
        xstage = esP1.enter_context(nc.sbuf_tensor("xstage", [128, 4, D], F32))
        xn_bf = esP1.enter_context(nc.sbuf_tensor("xn_bf", [128, 2, D], BF16))

        xslot_free = [None, None, None, None]
        xnslot_free = [None, None]
        NS = 4
        c = 1.0 / float(D)
        dS_tag = {}; sq_tag = {}; vchain_tag = {}; sqrt_tag = {}; nm2_tag = {}
        red_tag = {}
        apply_tag = {}; tt_tag = {}

        def sA(i):
            sl = i % NS
            qs = "scalar" if i % 2 == 0 else "gpsimd"
            sem = (("dA", "dB") if i % 2 == 0 else ("dV", "dC"))[(i // 2) % 2]
            dS_tag[i] = (sem, em.op(qs, lambda s, i=i, sl=sl: s.dma_start(
                out=xstage[:, sl, :], in_=x_d[i*128:(i+1)*128, :]),
                inc=sem, waits=[xslot_free[sl]]))

        def sB(i):
            sl = i % NS
            dA = dS_tag[i]
            red_tag[i] = em.op("vector", lambda v, i=i, sl=sl: v.reduce_sum(
                st_sum[:, i:i+1], xstage[:, sl, :], axis=AX.X),
                inc="V", waits=[dA])
            sq_tag[i] = em.op("scalar", lambda s, i=i, sl=sl: s.activation(
                xnT[:, :, i*128:(i+1)*128], xstage[:, sl, :], AF.Square,
                accum_out=st_sq[:, i:i+1]),
                inc="Sc", waits=[dA])

        def s23(i):
            m1 = em.op("vector", lambda v, i=i: v.tensor_scalar(
                st_mean[:, i:i+1], st_sum[:, i:i+1], c, None, op0=ALU.mult),
                inc="V", waits=[("Sc", sq_tag[i]), ("V", red_tag[i])])
            v2 = em.op("vector", lambda v, i=i: v.tensor_scalar(
                st_var[:, i:i+1], st_sq[:, i:i+1], c, None, op0=ALU.mult), inc="V")
            v3_ = em.op("vector", lambda v, i=i: v.tensor_tensor(
                st_nm[:, i:i+1], st_mean[:, i:i+1], st_mean[:, i:i+1], op=ALU.mult),
                inc="V", waits=[("V", m1)])
            vchain_tag[i] = em.op("vector", lambda v, i=i: v.tensor_tensor(
                st_var[:, i:i+1], st_var[:, i:i+1], st_nm[:, i:i+1], op=ALU.subtract),
                inc="V", waits=[("V", v3_)])
            sqrt_tag[i] = em.op("scalar", lambda s, i=i: s.activation(
                st_rstd[:, i:i+1], st_var[:, i:i+1], AF.Sqrt, bias=epsv[:]),
                inc="Sc", waits=[("V", vchain_tag[i]), ("G", gid)])

        def s45(i):
            v4_ = em.op("vector", lambda v, i=i: v.reciprocal(
                st_rstd[:, i:i+1], st_rstd[:, i:i+1]), inc="V",
                waits=[("Sc", sqrt_tag[i])])
            nm2_tag[i] = em.op("vector", lambda v, i=i: v.tensor_scalar(
                st_nm[:, i:i+1], st_mean[:, i:i+1], st_rstd[:, i:i+1], -1.0,
                op0=ALU.mult, op1=ALU.mult), inc="V", waits=[("V", v4_)])
            sl = i % NS
            nsl = i % 2
            wap = [("V", nm2_tag[i])]
            if xnslot_free[nsl]:
                wap.append(xnslot_free[nsl])
            apply_tag[i] = em.op("scalar", lambda s, i=i, sl=sl, nsl=nsl: s.activation(
                xn_bf[:, nsl, :], xstage[:, sl, :], AF.Identity,
                bias=st_nm[:, i:i+1], scale=st_rstd[:, i:i+1]),
                inc="Sc", waits=wap)
            xslot_free[sl] = ("Sc", apply_tag[i])

        def s6(i):
            nsl = i % 2
            tt = em.op("sync", lambda sy, i=i, nsl=nsl: sy.dma_start_transpose(
                xnT[:, :, i*128:(i+1)*128],
                xn_bf[:, nsl, :]),
                inc="dS", waits=[("Sc", apply_tag[i])])
            xnslot_free[nsl] = ("dS", tt)
            tt_tag[i] = tt

        for i in range(NT + 3):
            if i < NT:
                sA(i)
            if 0 <= i - 1 < NT:
                sB(i - 1)
            if 0 <= i - 2 < NT:
                s23(i - 2)
            if 0 <= i - 3 < NT:
                s45(i - 3)
                s6(i - 3)
        atag = apply_tag[NT - 1]
        xn_ready = ("dS", tt_tag[NT - 1])
        if debug:
            em.op("gpsimd", lambda g: g.dma_start(
                out=dbg_xnT, in_=xnT[:].rearrange("p c s -> p (c s)")),
                inc="dG", waits=[("V", em.cnt["V"])])

        if debug:
            em.op("gpsimd", lambda g: g.dma_start(out=dbg_st[:, 0:NT], in_=st_sum[:]),
                  inc="dG", waits=[("Sc", em.cnt["Sc"]), ("V", em.cnt["V"])])
            em.op("gpsimd", lambda g: g.dma_start(out=dbg_st[:, NT:2*NT], in_=st_sq[:]), inc="dG")
            em.op("gpsimd", lambda g: g.dma_start(out=dbg_st[:, 2*NT:3*NT], in_=st_mean[:]), inc="dG")
            em.op("gpsimd", lambda g: g.dma_start(out=dbg_st[:, 3*NT:4*NT], in_=st_var[:]), inc="dG")
            em.op("gpsimd", lambda g: g.dma_start(out=dbg_st[:, 4*NT:5*NT], in_=st_rstd[:]), inc="dG")
            em.op("gpsimd", lambda g: g.dma_start(out=dbg_st[:, 5*NT:6*NT], in_=st_nm[:]), inc="dG")
        p1_free = [("Sc", atag), xn_ready]
        block = es.enter_context(nc.Block())
        em.flush(block)
        esP1.close()

        stg_box = [None, None]
        qT = esB.enter_context(nc.sbuf_tensor("qT", [128, 2, S], BF16, side="right"))
        kT = esB.enter_context(nc.sbuf_tensor("kT", [128, 2, S], BF16, side="right"))
        vtm = esB.enter_context(nc.sbuf_tensor("vtm", [128, NT, 256], BF16, side="right"))
        expS = esB.enter_context(nc.sbuf_tensor("expS", [128, 3, 512], BF16, side="right"))
        expSm = esB.enter_context(nc.sbuf_tensor("expSm", [128, 2, 512], BF16, side="right"))
        rse = esB.enter_context(nc.sbuf_tensor("rse", [128, 512], F32, side="right"))
        stg2f = esB.enter_context(nc.sbuf_tensor("stg2f", [128, 3, 1024], F32, side="right"))
        stg2b = esB.enter_context(nc.sbuf_tensor("stg2b", [128, 2, 1024], BF16, side="right"))
        wd = {0: wq_d, 1: wk_d, 2: wv_d}
        wqkvT_free = [None, None, None]
        qkv_free = None
        last_xnT_use = None
        last_zT_write = None

        ds_base = em.cnt["dS"]
        def WXTAG(L):
            return ds_base + 16 * (L + 1)
        wload_tag = {}

        esA.close()
        esB.close()
        woT = es.enter_context(nc.sbuf_tensor("woT", [128, NH, D], BF16))
        fbB = es.enter_context(nc.sbuf_tensor("fbB", [128, 4, D], BF16))
        faT = es.enter_context(nc.sbuf_tensor("faT", [128, 4, DC, 128], BF16, side="right"))
        wrT = es.enter_context(nc.sbuf_tensor("wrT", [128, DC, NE], BF16, side="right"))
        xt = es.enter_context(nc.sbuf_tensor("xt", [128, 3, D], F32, side="right"))
        esS = contextlib.ExitStack()
        wr_stg = esS.enter_context(nc.sbuf_tensor("wr_stg", [8, 2048], BF16, side="right"))

        sc_box = [0]
        stream_k = [0]
        ld_tag = {}
        cast_tag = {}
        xbar_tag = {}

        def emit_stream_item():
            k = stream_k[0]
            stream_k[0] += 1
            sl = k % 3
            bsl = k % 2
            qs = "scalar" if k % 2 == 0 else "gpsimd"
            dsem = (("dA", "dB") if k % 2 == 0 else ("dV", "dC"))[(k // 2) % 2]
            xsem = "dS" if k % 2 == 0 else "dF"
            wld = [cast_tag[k - 3]] if k >= 3 else []
            if k < 32:
                q, h = k // 16, k % 16
                t = em.op(qs, lambda s, q=q, h=h, sl=sl: s.dma_start(
                    out=stg2f[:, sl, :].rearrange("p (c e) -> p c e", c=8),
                    in_=wo_d[h].rearrange("(c p) e -> p c e", p=128)[:, q*8:(q+1)*8, :]),
                    inc=dsem, waits=wld)
            elif k >= 40:
                er4, q = (k - 40) // 2, (k - 40) % 2
                t = em.op(qs, lambda s, er4=er4, q=q, sl=sl: s.dma_start(
                    out=stg2f[:, sl, :],
                    in_=fa_d[er4*128:(er4+1)*128, q*1024:(q+1)*1024]),
                    inc=dsem, waits=wld)
            else:
                kk = k - 32
                p_, q = kk // 2, kk % 2
                e0 = 2 * p_
                sv = stg2f[:, sl, :].rearrange("p (c a r) -> p c a r", c=8, a=2)
                em.op(qs, lambda s, sv=sv, e0=e0, q=q: s.dma_start(
                    out=sv[:, :, 0, :],
                    in_=fb_d[e0].rearrange("(c p) r -> p c r", p=128)[:, q*8:(q+1)*8, :]),
                    inc=dsem, waits=wld)
                t = em.op(qs, lambda s, sv=sv, e0=e0, q=q: s.dma_start(
                    out=sv[:, :, 1, :],
                    in_=fb_d[e0+1].rearrange("(c p) r -> p c r", p=128)[:, q*8:(q+1)*8, :]),
                    inc=dsem)
            ld_tag[k] = (dsem, t)
            wc = [ld_tag[k]]
            if k >= 2:
                wc.append(xbar_tag[k - 2])
            ct = em.op("vector", lambda v, sl=sl, bsl=bsl: v.tensor_copy(
                stg2b[:, bsl, :], stg2f[:, sl, :]), inc="V", waits=wc)
            cast_tag[k] = ("V", ct)
            wx = [("V", ct)]
            if k < 32:
                q, h = k // 16, k % 16
                wx.append(("T", last_xnT_use))
                xb = em.op("sync", lambda sy, q=q, h=h, bsl=bsl: sy.dma_start_transpose(
                    woT[:, h, q*1024:(q+1)*1024].rearrange("p (c e) -> p c e", c=8),
                    stg2b[:, bsl, :]), inc=xsem, waits=wx)
            elif k >= 40:
                er4, q = (k - 40) // 2, (k - 40) % 2
                wx.append(("T", sc_box[0]))
                xb = em.op("sync", lambda sy, er4=er4, q=q, bsl=bsl: sy.dma_start_transpose(
                    faT[:, er4, q*8:(q+1)*8, :], stg2b[:, bsl, :]),
                    inc=xsem, waits=wx)
            else:
                kk = k - 32
                p_, q = kk // 2, kk % 2
                wx += [("T", last_xnT_use), ("dS", WXTAG(47))]
                xb = em.op("sync", lambda sy, p_=p_, q=q, bsl=bsl: sy.dma_start_transpose(
                    fbB[:, p_, q*1024:(q+1)*1024].rearrange("p (c e) -> p c e", c=8),
                    stg2b[:, bsl, :]), inc=xsem, waits=wx)
            xbar_tag[k] = (xsem, xb)

        def emit_wloads(g_hg):
            for k in range(6):
                mi, h2 = k // 2, k % 2
                L = 6 * g_hg + k
                sl = L % 2
                w = [("dS", WXTAG(L - 2))] if L >= 2 else []
                wload_tag[(g_hg, mi, h2)] = em.op(
                    "gpsimd", lambda g, mi=mi, h2=h2, sl=sl, g_hg=g_hg:
                    g.dma_start(out=wstg[:, sl, :], in_=wd[mi][2*g_hg+h2]),
                    inc="dG", waits=w)

        emit_wloads(0)
        for hg in range(8):
            h0 = 2 * hg
            tp_tags = {}
            for k in range(6):
                mi, h2 = k // 2, k % 2
                L = 6 * hg + k
                sl = L % 2
                wv_ = [("dG", wload_tag[(hg, mi, h2)])]
                if wqkvT_free[mi] is not None and h2 == 0:
                    wv_.append(wqkvT_free[mi])
                ttag = em.op("sync", lambda sy, mi=mi, h2=h2, sl=sl:
                             sy.dma_start_transpose(
                                 wqkvT[:, mi, h2, :, :],
                                 wstg[:, sl, :]),
                             inc="dS", waits=wv_)
                assert ttag == WXTAG(L), (ttag, WXTAG(L))
                tp_tags[mi] = ("dS", ttag)
            if hg < 7:
                emit_wloads(hg + 1)
            else:
                pass

            qk_evac = {}
            for mi in range(2):
                dst = qT if mi == 0 else kT
                for h2 in range(2):
                    for j in range(NJ):
                        b = j % 2
                        mm = None
                        for dc in range(DC):
                            w = (bw(b) + [tp_tags[mi], qkv_free, xn_ready]) if dc == 0 else []
                            mm = em.op("tensor", lambda tn, mi=mi, h2=h2, j=j, dc=dc, b=b:
                                       tn.matmul(ps[b][:, 0:512],
                                                 wqkvT[:, mi, h2, dc, :],
                                                 xnT[:, dc, j*512:(j+1)*512],
                                                 start=(dc == 0), stop=(dc == DC-1)),
                                       inc="T" if dc == DC-1 else None, waits=w)
                        last_xnT_use = mm
                        wqkvT_free[mi] = ("T", mm)
                        vt = em.op("vector", lambda v, dst=dst, h2=h2, j=j, b=b:
                                   v.tensor_copy(dst[:, h2, j*512:(j+1)*512],
                                                 ps[b][:, 0:512]),
                                   inc="V", waits=[("T", mm)])
                        bank_free[b] = ("V", vt)
                        qk_evac[(mi, h2, j)] = vt
            v_evac = {}
            for mt in range(NT):
                b = mt % 2
                mm = None
                for dc in range(DC):
                    w = (bw(b) + [tp_tags[2], qkv_free, xn_ready]) if dc == 0 else []
                    mm = em.op("tensor", lambda tn, mt=mt, dc=dc, b=b:
                               tn.matmul(ps[b][:, 0:256],
                                         xnT[:, dc, mt*128:(mt+1)*128],
                                         wqkvT[:, 2, :, dc, :],
                                         start=(dc == 0), stop=(dc == DC-1)),
                               inc="T" if dc == DC-1 else None, waits=w)
                last_xnT_use = mm
                wqkvT_free[2] = ("T", mm)
                vt = em.op("vector", lambda v, mt=mt, b=b:
                           v.tensor_copy(vtm[:, mt, :], ps[b][:, 0:256]),
                           inc="V", waits=[("T", mm)])
                bank_free[b] = ("V", vt)
                v_evac[mt] = vt

            for h2 in range(2):
                h = h0 + h2
                for j in range(NJ):
                    if hg == 7:
                        for _ in range(5):
                            emit_stream_item()
                    nmc = 4 * (j + 1)
                    bZ = 4 + (j % 2)
                    bSE = 6 + (j % 2)
                    slot_rel = {}
                    se_tags = {}
                    pv_last = None
                    sbanks = [2, 3, 0, 1]
                    stags = {}

                    def emit_score(mc, h2=h2, j=j):
                        bS = sbanks[mc % 4]
                        w = bw(bS)
                        if mc == 0:
                            w += [("V", qk_evac[(0, h2, j)]), ("V", qk_evac[(1, h2, j)])]
                        stags[mc] = (bS, em.op(
                            "tensor", lambda tn, h2=h2, j=j, mc=mc, bS=bS:
                            tn.matmul(ps[bS][:, 0:512],
                                      kT[:, h2, mc*128:(mc+1)*128],
                                      qT[:, h2, j*512:(j+1)*512],
                                      start=True, stop=True),
                            inc="T", waits=w))
                        if hg == 7:
                            sc_box[0] = stags[mc][1]

                    for pre in range(min(3, nmc)):
                        emit_score(pre)
                    for mc in range(nmc):
                        bS, stag = stags[mc]
                        esl = mc % 3
                        we = [("T", stag)]
                        if mc >= 3:
                            we.append(slot_rel[mc - 3])
                        et = em.op("scalar", lambda s, bS=bS, esl=esl:
                                   s.activation(expS[:, esl, :], ps[bS][:, 0:512],
                                                AF.Exp, scale=INVSQ, bias=negC[:]),
                                   inc="Sc", waits=we + [("G", gid)])
                        bank_free[bS] = ("Sc", et)
                        diag = (mc >= 4 * j)
                        if diag:
                            msl = mc % 2
                            wg = [("Sc", et)]
                            if mc - 2 >= 4 * j:
                                wg.append(("T", se_tags[mc - 2]))
                            gt = em.op("gpsimd", lambda g, esl=esl, msl=msl, j=j, mc=mc:
                                       g.affine_select(expSm[:, msl, :], expS[:, esl, :],
                                                       [[1, 512]], ALU.is_ge, 0.0,
                                                       base=512*j - 128*mc,
                                                       channel_multiplier=-1),
                                       inc="G", waits=wg)
                            ready = ("G", gt)
                            srcap = expSm[:, msl, :]
                            slot_rel[mc] = ("G", gt)
                        else:
                            ready = ("Sc", et)
                            srcap = expS[:, esl, :]
                            slot_rel[mc] = None
                        if debug and hg == 0 and h2 == 0 and j == 0:
                            dsem = em.op("gpsimd", lambda g, srcap=srcap, mc=mc:
                                         g.dma_start(out=dbg_exp[:, mc*512:(mc+1)*512],
                                                     in_=srcap), inc="dG", waits=[ready])
                        if mc + 3 < nmc:
                            emit_score(mc + 3)
                        wpv = [ready, ("V", v_evac[mc])]
                        if mc == 0:
                            wpv += bw(bZ)
                        pv_last = em.op("tensor",
                                        lambda tn, h2=h2, mc=mc, bZ=bZ, srcap=srcap, nmc=nmc:
                                        tn.matmul(ps[bZ][:, 0:512],
                                                  vtm[:, mc, h2*128:(h2+1)*128],
                                                  srcap,
                                                  start=(mc == 0), stop=(mc == nmc-1)),
                                        inc="T", waits=wpv)
                        wse = bw(bSE) if mc == 0 else []
                        se = em.op("tensor", lambda tn, mc=mc, bSE=bSE, srcap=srcap, nmc=nmc:
                                   tn.matmul(ps[bSE][:, 0:512],
                                             onesq[:],
                                             srcap,
                                             start=(mc == 0), stop=(mc == nmc-1)),
                                   inc="T", waits=wse)
                        se_tags[mc] = se
                        if not diag:
                            slot_rel[mc] = ("T", se)
                    wrt = [("T", se_tags[nmc-1])]
                    if last_zT_write is not None:
                        wrt.append(("V", last_zT_write))
                    rt = em.op("vector", lambda v, bSE=bSE: v.reciprocal(
                        rse[:], ps[bSE][:, 0:512]),
                        inc="V", waits=wrt)
                    if debug and hg == 0 and h2 == 0 and j == 0:
                        em.op("gpsimd", lambda g: g.dma_start(
                            out=dbg_rse, in_=rse[:]), inc="dG",
                            waits=[("V", rt)])
                    bank_free[bSE] = ("V", rt)
                    em.op("vector", lambda v: v.engine_nop())
                    zt = em.op("vector", lambda v, h=h, j=j, bZ=bZ: v.tensor_tensor(
                        zT[:, h, j*512:(j+1)*512], ps[bZ][:, 0:512],
                        rse[:], op=ALU.mult),
                        inc="V", waits=[("T", pv_last), ("V", rt)])
                    bank_free[bZ] = ("V", zt)
                    last_zT_write = zt
            qkv_free = None

        tag_A_free = ("T", last_xnT_use)
        p2_T_final = em.cnt["T"]
        p2_Sc_final = em.cnt["Sc"]
        tag_B_free = ("V", em.cnt["V"])
        if debug:
            em.op("gpsimd", lambda g: g.dma_start(
                out=dbg_zT, in_=zT[:].rearrange("p c s -> p (c s)")),
                inc="dG", waits=[("V", em.cnt["V"])])

        em.flush(block)

        while stream_k[0] < 48:
            emit_stream_item()
        wo_ready_q = [xbar_tag[15], xbar_tag[31]]
        fb_ready = xbar_tag[39]
        fa_ready = xbar_tag[47]

        early_x = {}
        for stq in range(2):
            early_x[stq] = em.op("gpsimd", lambda g, stq=stq: g.dma_start(
                out=xt[:, stq, :], in_=x_d[stq*128:(stq+1)*128, :]),
                inc="dG", waits=[tag_B_free, xbar_tag[46], xbar_tag[47]])

        wr_dt = em.op("gpsimd", lambda g: g.dma_start(
            out=wr_stg[:], in_=wr_d), inc="dG",
            waits=[xbar_tag[46], xbar_tag[47]])
        pvw = ps[2]
        ttw = None
        for dc in range(DC):
            w2 = (bw(2) + [("dG", wr_dt), ("G", gid)]) if dc == 0 else []
            ttw = em.op("tensor", lambda tn, dc=dc: tn.transpose(
                pvw[:, dc*4:dc*4+4].bitcast(BF16),
                wr_stg[:, dc*128:(dc+1)*128], idb[0:8, 0:8]),
                inc="T" if dc == DC-1 else None, waits=w2)
        vtw = em.op("vector", lambda v: v.tensor_copy(
            wrT[:], pvw[:, 0:64].bitcast(BF16).rearrange("p (c e) -> p c e", c=DC)),
            inc="V", waits=[("T", ttw)])
        bank_free[2] = ("V", vtw)
        wr_ready = ("V", vtw)

        em.flush(block)
        esS.close()
        p3_region = [("T", ttw)]

        g_bf = es.enter_context(nc.sbuf_tensor("g_bf", [128, D], BF16, side="right"))
        gT = es.enter_context(nc.sbuf_tensor("gT", [128, 2, DC, 128], BF16, side="right"))
        lg = es.enter_context(nc.sbuf_tensor("lg", [128, 2, NE], F32, side="right"))
        expl = es.enter_context(nc.sbuf_tensor("expl", [128, NE], F32, side="right"))
        lmax = es.enter_context(nc.sbuf_tensor("lmax", [128, 1], F32, side="right"))
        emax = es.enter_context(nc.sbuf_tensor("emax", [128, 1], F32, side="right"))
        sume = es.enter_context(nc.sbuf_tensor("sume", [128, 1], F32, side="right"))
        wsc = es.enter_context(nc.sbuf_tensor("wsc", [128, 1], F32, side="right"))
        maskw = es.enter_context(nc.sbuf_tensor("maskw", [128, 2, NE], F32, side="right"))
        hm_bf = es.enter_context(nc.sbuf_tensor("hm_bf", [128, ER], BF16, side="right"))
        hmT = es.enter_context(nc.sbuf_tensor("hmT", [128, 4, 128], BF16, side="right"))

        xt_free = [None, None, None]
        g_bf_free = None
        gT_free = [None, None]
        hm_free = None
        hmT_free = None

        zt_ready = ("V", last_zT_write)
        attn_mm = {}
        attn_dma = {}
        attn_evac_last = {}
        out_last = {}

        def emit_attn_T(st, dmcs=(0, 1, 2, 3)):
            sl = st % 3
            if dmcs[0] == 0:
                if st < 2:
                    attn_dma[st] = early_x[st]
                else:
                    wx = [xt_free[sl]]
                    attn_dma[st] = em.op("gpsimd", lambda g, st=st, sl=sl: g.dma_start(
                        out=xt[:, sl, :], in_=x_d[st*128:(st+1)*128, :]),
                        inc="dG", waits=wx)
            for dmc in dmcs:
                b = dmc
                mm = None
                for h in range(NH):
                    w = (bw(b) + [zt_ready, wo_ready_q[dmc // 2]]) if h == 0 else []
                    mm = em.op("tensor", lambda tn, st=st, dmc=dmc, h=h, b=b:
                               tn.matmul(ps[b][:, 0:512],
                                         zT[:, h, st*128:(st+1)*128],
                                         woT[:, h, dmc*512:(dmc+1)*512],
                                         start=(h == 0), stop=(h == NH-1)),
                               inc="T" if h == NH-1 else None, waits=w)
                attn_mm[(st, dmc)] = mm

        def emit_attn_evacs(st):
            sl = st % 3
            ev = None
            for dmc in range(4):
                b = dmc
                ev = em.op("vector", lambda v, sl=sl, dmc=dmc, b=b:
                           v.tensor_tensor(xt[:, sl, dmc*512:(dmc+1)*512],
                                           ps[b][:, 0:512],
                                           xt[:, sl, dmc*512:(dmc+1)*512],
                                           op=ALU.add),
                           inc="V", waits=[("T", attn_mm[(st, dmc)]),
                                           ("dG", attn_dma[st])])
                bank_free[b] = ("V", ev)
            attn_evac_last[st] = ("V", ev)

        gT_ready = {}
        mw_tag = {}

        def emit_front(st):
            nonlocal g_bf_free
            sl = st % 3
            gsl = st % 2
            sq_box[0] = g_bf
            x1_ready = [attn_evac_last[st]]
            if st == 0:
                x1_ready += p3_region
            if g_bf_free:
                x1_ready = x1_ready + [g_bf_free]
            atag = emit_ln(xt[:, sl, :], st, g_bf[:], x1_ready,
                           extra_out_wait=g_bf_free)
            wg_ = [("Sc", atag)]
            if gT_free[gsl]:
                wg_.append(gT_free[gsl])
            elif st == 0:
                wg_ += p3_region
            gvt = em.op("sync", lambda sy, gsl=gsl: sy.dma_start_transpose(
                gT[:, gsl, :, :], g_bf[:]), inc="dS", waits=wg_)
            g_bf_free = ("dS", gvt)
            gT_ready[st] = ("dS", gvt)
            mm = None
            for dc in range(DC):
                w = (bw(4) + [gT_ready[st], wr_ready]) if dc == 0 else []
                mm = em.op("tensor", lambda tn, dc=dc, gsl=gsl:
                           tn.matmul(ps[4][:, 0:NE], gT[:, gsl, dc, :], wrT[:, dc, :],
                                     start=(dc == 0), stop=(dc == DC-1)),
                           inc="T" if dc == DC-1 else None, waits=w)
            msl = st % 2
            vtag = em.op("vector", lambda v, msl=msl: v.tensor_tensor(
                lg[:, msl, :], ps[4][:, 0:NE],
                br_sb[:], op=ALU.add),
                inc="V", waits=[("T", mm), ("dS", brt)])
            bank_free[4] = ("V", vtag)
            vl = em.op("vector", lambda v, msl=msl: v.tensor_reduce(
                lmax[:], lg[:, msl, :], axis=AX.X, op=ALU.max), inc="V",
                waits=[("V", vtag)])
            etag = em.op("scalar", lambda s, msl=msl: s.activation(
                expl[:], lg[:, msl, :], AF.Exp, bias=negC[:]), inc="Sc",
                waits=[("V", vl), ("G", gid)])
            v5 = em.op("vector", lambda v: v.reduce_sum(
                sume[:], expl[:], axis=AX.X), inc="V", waits=[("Sc", etag)])
            v6 = em.op("vector", lambda v: v.tensor_reduce(
                emax[:], expl[:], axis=AX.X, op=ALU.max), inc="V", waits=[("V", v5)])
            v7 = em.op("vector", lambda v: v.reciprocal(sume[:], sume[:]),
                       inc="V", waits=[("V", v6)])
            v8 = em.op("vector", lambda v: v.tensor_scalar(
                wsc[:], emax[:], sume[:], 1.0/64.0, op0=ALU.mult, op1=ALU.mult),
                inc="V", waits=[("V", v7)])
            mw = em.op("vector", lambda v, msl=msl: v.tensor_scalar(
                maskw[:, msl, :], lg[:, msl, :], lmax[:], wsc[:],
                op0=ALU.is_ge, op1=ALU.mult),
                inc="V", waits=[("V", v8)])
            mw_tag[st] = ("V", mw)

        def emit_back(st):
            nonlocal hm_free, hmT_free
            sl = st % 3
            gsl = st % 2
            msl = st % 2
            mmh = None
            for dc in range(DC):
                w = (bw(5) + [gT_ready[st], fa_ready]) if dc == 0 else []
                mmh = em.op("tensor", lambda tn, dc=dc, gsl=gsl:
                            tn.matmul(ps[5][:, 0:ER], gT[:, gsl, dc, :],
                                      faT[:, :, dc, :],
                                      start=(dc == 0), stop=(dc == DC-1)),
                            inc="T" if dc == DC-1 else None, waits=w)
            gT_free[gsl] = ("T", mmh)
            hev = None
            for e in range(NE):
                w = [("T", mmh), mw_tag[st]]
                if e == 0 and hm_free is not None:
                    w.append(hm_free)
                hev = em.op("vector", lambda v, e=e, msl=msl: v.tensor_scalar(
                    hm_bf[:, e*64:(e+1)*64], ps[5][:, e*64:(e+1)*64],
                    maskw[:, msl, e:e+1], None, op0=ALU.mult),
                    inc="V" if e == NE-1 else None, waits=w)
            bank_free[5] = ("V", hev)
            tt = None
            for k in range(4):
                w2 = (bw(4) + [("V", hev)] + ([hmT_free] if hmT_free else [])) if k == 0 else []
                tt = em.op("tensor", lambda tn, k=k:
                           tn.transpose(ps[4][:, k*64:k*64+64].bitcast(BF16),
                                        hm_bf[:, k*128:(k+1)*128], idb[:]),
                           inc="T" if k == 3 else None, waits=w2)
            hm_free = ("T", tt)
            hvt = em.op("vector", lambda v: v.tensor_copy(
                hmT[:], ps[4][:, 0:256].bitcast(BF16).rearrange("p (c e) -> p c e", c=4)),
                inc="V", waits=[("T", tt)])
            bank_free[4] = ("V", hvt)
            hmT_ready = ("V", hvt)
            out_tags = []
            mm = None
            for dmc in range(4):
                b = 6 + dmc % 2
                for erc in range(4):
                    w = (bw(b) + [hmT_ready, fb_ready]) if erc == 0 else []
                    mm = em.op("tensor", lambda tn, dmc=dmc, erc=erc, b=b:
                               tn.matmul(ps[b][:, 0:512],
                                         hmT[:, erc, :],
                                         fbB[:, erc, dmc*512:(dmc+1)*512],
                                         start=(erc == 0), stop=(erc == 3)),
                               inc="T" if erc == 3 else None, waits=w)
                yev = em.op("vector", lambda v, sl=sl, dmc=dmc, b=b:
                            v.tensor_tensor(xt[:, sl, dmc*512:(dmc+1)*512],
                                            ps[b][:, 0:512],
                                            xt[:, sl, dmc*512:(dmc+1)*512],
                                            op=ALU.add),
                            inc="V", waits=[("T", mm)])
                bank_free[b] = ("V", yev)
                ot = em.op("gpsimd", lambda g, st=st, sl=sl, dmc=dmc: g.dma_start(
                    out=out_d[st*128:(st+1)*128, dmc*512:(dmc+1)*512],
                    in_=xt[:, sl, dmc*512:(dmc+1)*512]),
                    inc="dO", waits=[("V", yev)])
                out_tags.append(ot)
            hmT_free = ("T", mm)
            xt_free[sl] = ("dO", out_tags[-1])

        for k in range(NT + 1):
            if k < NT:
                emit_attn_T(k)
            if 1 <= k <= NT:
                emit_front(k - 1)
                emit_back(k - 1)
            if k < NT:
                emit_attn_evacs(k)

        final_dO = em.cnt["dO"]
        em.op("scalar", lambda s: s.nop(), waits=[("dO", final_dO)])
        em.op("sync", lambda sy: sy.nop(), waits=[("dO", final_dO)])
        em.flush(block)

    return nc


def kernel(L0_attn_x, L0_attn_W_Q, L0_attn_W_K, L0_attn_W_V, L0_attn_W_O,
           L0_ffn_A, L0_ffn_B, L0_ffn_Wr, L0_ffn_br):
    nc = build_program(debug=DEBUG)
    x = np.ascontiguousarray(L0_attn_x, dtype=np.float32)
    fa = np.ascontiguousarray(np.asarray(L0_ffn_A, dtype=np.float32).reshape(ER, D))
    shared = {
        "wq": np.ascontiguousarray(L0_attn_W_Q, dtype=np.float32),
        "wk": np.ascontiguousarray(L0_attn_W_K, dtype=np.float32),
        "wv": np.ascontiguousarray(L0_attn_W_V, dtype=np.float32),
        "wo": np.ascontiguousarray(L0_attn_W_O, dtype=np.float32),
        "fa": fa,
        "fb": np.ascontiguousarray(L0_ffn_B, dtype=np.float32),
        "wr": np.ascontiguousarray(L0_ffn_Wr, dtype=np.float32),
        "br": np.ascontiguousarray(np.asarray(L0_ffn_br, dtype=np.float32).reshape(1, NE)),
    }
    in_maps = [dict(shared, x=np.ascontiguousarray(x[c])) for c in range(8)]
    res = run_bass_kernel_spmd(nc, in_maps, core_ids=list(range(8)))
    return np.stack([r["out"] for r in res.results], axis=0)
